# version 02
# baseline (speedup 1.0000x reference)
import numpy as np
from contextlib import ExitStack

import concourse.bass as bass
import concourse.mybir as mybir
from concourse.bass_utils import run_bass_kernel_spmd

F32 = mybir.dt.float32
BF16 = mybir.dt.bfloat16
I32 = mybir.dt.int32
ALU = mybir.AluOpType
AF = mybir.ActivationFunctionType
ds = bass.ds

NC = 8
DM = 1024
DI = 2048
DS = 16
HID = 3072
L = 16384
TB = L // NC
EX = 64
TA = TB + EX
DSL = DI // NC
LNEPS = 1e-5
QSCALE = 1.0 / 127.0

ACH = [(0, 512), (512, 512), (1024, 512), (1536, 512), (2048, 64)]
SCT = 1024
NSC = L // SCT


class Ctr:
    def __init__(self, h):
        self.h = h
        self.n = 0

    def bump(self, amt):
        self.n += amt
        return self.n


class Sched:
    ENGS = ("sync", "gpsimd", "vector", "scalar", "tensor")

    def __init__(self):
        self.ops = {e: [] for e in self.ENGS}

    def add(self, eng, fn, waits=(), inc=None):
        waits = [(c, v) for (c, v) in waits if v > 0]
        for c, v in waits:
            assert v <= c.n, f"wait on future count {v} > {c.n}"
        entry = [fn, waits, None]
        if inc is not None:
            c, amt = inc
            c.bump(amt)
            entry[2] = (c, amt)
        self.ops[eng].append(entry)

    def emit(self, eng_obj, eng_name):
        for fn, waits, inc in self.ops[eng_name]:
            for c, v in waits:
                eng_obj.wait_ge(c.h, v)
            ins = fn(eng_obj)
            if inc is not None:
                c, amt = inc
                assert ins is not None
                ins.then_inc(c.h, amt)


DEBUG = False


def build_nc():
    nc = bass.Bass("TRN2", target_bir_lowering=False,
                   detect_race_conditions=False)

    xt = nc.dram_tensor("xt", [DM, TA], F32, kind="ExternalInput")
    wit = nc.dram_tensor("wit", [32 * 8 * 128, 128], F32, kind="ExternalInput")
    wxt = nc.dram_tensor("wxt", [DI, 80], F32, kind="ExternalInput")
    cwt = nc.dram_tensor("cwt", [DI, 4], F32, kind="ExternalInput")
    cbt = nc.dram_tensor("cbt", [DI, 1], F32, kind="ExternalInput")
    dpar = nc.dram_tensor("dpar", [DI, 1], F32, kind="ExternalInput")
    wdt = nc.dram_tensor("wdt", [DSL, 1], F32, kind="ExternalInput")
    bdt = nc.dram_tensor("bdt", [DSL, 1], F32, kind="ExternalInput")
    alog = nc.dram_tensor("alog", [DSL, DS], F32, kind="ExternalInput")
    wg_t = nc.dram_tensor("wg_t", [24 * 8 * 128, 128], I32, kind="ExternalInput")
    wu_t = nc.dram_tensor("wu_t", [24 * 8 * 128, 128], I32, kind="ExternalInput")
    wd_t = nc.dram_tensor("wd_t", [8 * 24 * 128, 128], I32, kind="ExternalInput")
    wot = nc.dram_tensor("wot", [8 * 16 * 128, 128], F32, kind="ExternalInput")
    ones_c = nc.dram_tensor("ones_c", [128, 1], F32, kind="ExternalInput")
    ones_r = nc.dram_tensor("ones_r", [1, 128], F32, kind="ExternalInput")
    ident = nc.dram_tensor("ident", [128, 128], F32, kind="ExternalInput")
    oneh = nc.dram_tensor("oneh", [16, 16 * 128], F32, kind="ExternalInput")
    cvec = nc.dram_tensor("cvec", [1, 1], I32, kind="ExternalInput")
    ot = nc.dram_tensor("ot", [DM, TB], F32, kind="ExternalOutput")
    if DEBUG:
        dbg_xcv = nc.dram_tensor("dbg_xcv", [DI, TB], BF16,
                                 kind="ExternalOutput")
        dbg_zs = nc.dram_tensor("dbg_zs", [DI, TB], BF16,
                                kind="ExternalOutput")
        dbg_xp = nc.dram_tensor("dbg_xp", [80, TB], F32,
                                kind="ExternalOutput")
        dbg_y = nc.dram_tensor("dbg_y", [DSL, L], BF16, kind="ExternalOutput")
        dbg_g = nc.dram_tensor("dbg_g", [128, 8 * TB], BF16,
                               kind="ExternalOutput")
        dbg_hf = nc.dram_tensor("dbg_hf", [128, 24 * 1024], BF16,
                                kind="ExternalOutput")
        dbg_hf0 = nc.dram_tensor("dbg_hf0", [128, 24 * 1024], BF16,
                                 kind="ExternalOutput")
        dbg_gate = nc.dram_tensor("dbg_gate", [128, 1024], F32,
                                  kind="ExternalOutput")
        dbg_g0 = nc.dram_tensor("dbg_g0", [128, 8 * TB], BF16,
                                kind="ExternalOutput")
        dbg_g1 = nc.dram_tensor("dbg_g1", [128, 8 * TB], BF16,
                                kind="ExternalOutput")
        dbg_up = nc.dram_tensor("dbg_up", [128, 1024], F32,
                                kind="ExternalOutput")
        dbg_wg = nc.dram_tensor("dbg_wg", [128, 1024], BF16,
                                kind="ExternalOutput")
        dbg_dt0 = nc.dram_tensor("dbg_dt0", [128, SCT], F32,
                                 kind="ExternalOutput")
        dbg_a0 = nc.dram_tensor("dbg_a0", [128, SCT], BF16,
                                kind="ExternalOutput")
        dbg_bb0 = nc.dram_tensor("dbg_bb0", [128, SCT], BF16,
                                 kind="ExternalOutput")
        dbg_u0 = nc.dram_tensor("dbg_u0", [128, SCT], BF16,
                                kind="ExternalOutput")
        dbg_h0 = nc.dram_tensor("dbg_h0", [128, SCT], BF16,
                                kind="ExternalOutput")

    xcv_in = nc.dram_tensor("xcv_in", [DI, TB], BF16)
    xcv_g = nc.dram_tensor("xcv_g", [NC * DI, TB], BF16)
    xp_in = nc.dram_tensor("xp_in", [80, TB], F32)
    xp_g = nc.dram_tensor("xp_g", [NC * 80, TB], F32)
    y_in = nc.dram_tensor("y_in", [DSL, L], BF16)
    y_g = nc.dram_tensor("y_g", [NC * DSL, L], BF16)
    zs_pri = nc.dram_tensor("zs_pri", [DI, TB], BF16)
    u_pri = nc.dram_tensor("u_pri", [NC * DSL, TB], BF16)
    y_pri = nc.dram_tensor("y_pri", [DI, TB], BF16)
    wgbf = nc.dram_tensor("wgbf", [24 * 8 * 128, 128], BF16)
    wubf = nc.dram_tensor("wubf", [24 * 8 * 128, 128], BF16)
    wdbf = nc.dram_tensor("wdbf", [8 * 24 * 128, 128], BF16)

    sch = Sched()

    with ExitStack() as gctx:
        SBg = lambda n, s, d: gctx.enter_context(nc.sbuf_tensor(n, s, d))
        DX = Ctr(gctx.enter_context(nc.semaphore("DX")))
        DG = Ctr(gctx.enter_context(nc.semaphore("DG")))
        TM = Ctr(gctx.enter_context(nc.semaphore("TM")))
        SV = Ctr(gctx.enter_context(nc.semaphore("SV")))
        VV = Ctr(gctx.enter_context(nc.semaphore("VV")))
        CC = Ctr(gctx.enter_context(nc.semaphore("CC")))

        quad0 = gctx.enter_context(nc.psum_tensor("quad0", [128, 2048], F32))
        quad1 = gctx.enter_context(nc.psum_tensor("quad1", [128, 2048], F32))
        cw_sb = SBg("cw_sb", [128, 16, 4], F32)
        cb_sb = SBg("cb_sb", [128, 16], F32)
        dp_sb = SBg("dp_sb", [128, 16], F32)
        wdt_sb = SBg("wdt_sb", [128, 2], F32)
        bdt_sb = SBg("bdt_sb", [128, 2], F32)
        an_sb = SBg("an_sb", [128, 32], F32)
        onec_sb = SBg("onec_sb", [128, 1], F32)
        oner_sb = SBg("oner_sb", [1, 128], F32)
        id_f = SBg("id_f", [128, 128], F32)
        id_bf = SBg("id_bf", [128, 128], BF16)
        oneh_f = SBg("oneh_f", [16, 16 * 128], F32)
        oneh_bf = SBg("oneh_bf", [16, 16 * 128], BF16)
        cv_sb = SBg("cv_sb", [1, 1], I32)
        alog_st = SBg("alog_st", [128, 2, DS], F32)
        wx_f = SBg("wx_f", [128, 16 * 80], F32)
        wx_bf = SBg("wx_bf", [128, 16 * 80], BF16)
        eps_sb = SBg("eps_sb", [128, 1], F32)
        g_sb = SBg("g_sb", [128, 8 * TB], BF16)

        def dma(e, out_ap, in_ap):
            return e.dma_start(out=out_ap, in_=in_ap)

        small = [
            (cw_sb[:, :, :], cwt.ap().rearrange("(a p) k -> p a k", p=128)),

            (onec_sb[:, :], ones_c[:, :]),
            (oner_sb[:, :], ones_r[:, :]),
            (id_f[:, :], ident[:, :]),
            (oneh_f[:, :], oneh[:, :]),
            (cv_sb[:, :], cvec[:, :]),
            (alog_st[:, :, :], alog.ap().rearrange("(a p) n -> p a n", p=128)),
            (wx_f[:, :].rearrange("p (a k) -> p a k", k=80),
             wxt.ap().rearrange("(a p) k -> p a k", p=128)),
        ]
        for dst, src in small:
            sch.add("sync", lambda e, d=dst, s=src: dma(e, d, s), inc=(DX, 16))
        for colsb, coldr, na in ((cb_sb, cbt, 16), (dp_sb, dpar, 16),
                                 (wdt_sb, wdt, 2), (bdt_sb, bdt, 2)):
            for a in range(na):
                sch.add("sync", lambda e, d=colsb[:, a:a + 1],
                        s=coldr.ap()[a * 128:(a + 1) * 128, :]:
                        dma(e, d, s), inc=(DX, 16))
        init_dx = DX.n

        sch.add("vector", lambda e: e.tensor_copy(id_bf[:, :], id_f[:, :]),
                waits=[(DX, init_dx)])
        sch.add("vector", lambda e: e.tensor_copy(oneh_bf[:, :], oneh_f[:, :]))
        sch.add("vector", lambda e: e.tensor_copy(wx_bf[:, :], wx_f[:, :]))
        sch.add("vector", lambda e: e.memset(eps_sb[:, :], LNEPS),
                inc=(VV, 1))
        sch.add("scalar", lambda e: e.activation(
            an_sb[:, :], alog_st[:, :, :].rearrange("p a n -> p (a n)"),
            AF.Exp), waits=[(DX, init_dx)])
        sch.add("scalar", lambda e: e.mul(an_sb[:, :], an_sb[:, :], -1.0),
                inc=(SV, 1))

        ictx = ExitStack()
        if True:
            SBi = lambda n, s, d: ictx.enter_context(nc.sbuf_tensor(n, s, d))
            qi = [SBi(f"qi{i}", [128, 2048], I32) for i in range(2)]
            qf = [SBi(f"qf{i}", [128, 2048], BF16) for i in range(2)]
            steps = []
            for wsrc, wdst in ((wg_t, wgbf), (wu_t, wubf), (wd_t, wdbf)):
                for blk in range(12):
                    steps.append((wsrc, wdst, blk))
            dq_store = {}
            dq_cast = {}
            for i, (wsrc, wdst, blk) in enumerate(steps):
                b = i % 2
                src = wsrc.ap()[ds(blk * 2048, 2048), :].rearrange(
                    "(a p) c -> p a c", p=128)
                dst = wdst.ap()[ds(blk * 2048, 2048), :].rearrange(
                    "(a p) c -> p a c", p=128)
                w_in = [(VV, dq_cast.get(i - 2, 0))]
                sch.add("sync", lambda e, s=src, b=b: dma(
                    e, qi[b][:, :].rearrange("p (a c) -> p a c", c=128), s),
                        waits=w_in, inc=(DX, 16))
                ld = DX.n
                sch.add("vector", lambda e, b=b: e.tensor_scalar(
                    qf[b][:, :], qi[b][:, :], QSCALE, None, ALU.mult),
                    waits=[(DX, ld), (DG, dq_store.get(i - 2, 0))],
                    inc=(VV, 1))
                dq_cast[i] = VV.n
                sch.add("gpsimd", lambda e, d=dst, b=b: dma(
                    e, d, qf[b][:, :].rearrange("p (a c) -> p a c", c=128)),
                        waits=[(VV, dq_cast[i])], inc=(DG, 16))
                dq_store[i] = DG.n
            dequant_dg = DG.n
            last_dq_cast = dq_cast[len(steps) - 1]
            ictx.close()

            xnctx = ExitStack()
            xn_sb = xnctx.enter_context(
                nc.sbuf_tensor("xn_sb", [128, 8 * TA], BF16))
            with ExitStack() as actx:
                SBa = lambda n, s, d: actx.enter_context(nc.sbuf_tensor(n, s, d))
                xt_sb = [SBa("xt_sb0", [128, 8 * 512], F32)]
                xsq = SBa("xsq", [128, 8 * 512], F32)
                mrow = SBa("mrow", [1, 512], F32)
                vrow = SBa("vrow", [1, 512], F32)
                rrow = SBa("rrow", [1, 512], F32)
                tmp_f = SBa("tmp_f", [128, 512], F32)
                wi_sb = [SBa(f"wi_sb{i}", [128, 8 * 128], F32) for i in range(2)]
                wib_sb = [SBa(f"wib_sb{i}", [128, 8 * 128], BF16)
                          for i in range(2)]
                xin_sb = SBa("xin_sb", [128, 16 * 512], BF16)
                xcv_sb = SBa("xcv_sb", [128, 16 * 512], BF16)
                zs_sb = SBa("zs_sb", [128, 16 * 512], BF16)
                xe_sb = SBa("xe_sb", [128, 520], BF16)
                acc_sb = [SBa(f"acc_sb{i}", [128, 512], F32) for i in range(2)]
                tail_sb = SBa("tail_sb", [128, 16 * 3], BF16)
                xp_sb = SBa("xp_sb", [80, 512], F32)

                sch.add("vector", lambda e: e.memset(tail_sb[:, :], 0.0))

                sumsq_tm = {}
                sq_sv_d = {}
                evac_sv = {}
                wicast_sv = {}
                conv_vv = {}
                silu_sv = {}
                xnlast_vv = {}
                convlast_vv = {}
                xp_sv_d = {}
                sp_dg = {}
                x_loaded = {}
                wimm_tm = {}

                last_spill = 0
                for ci, (c0, w) in enumerate(ACH):
                    xb = 0
                    src = xt.ap()[:, c0:c0 + w].rearrange(
                        "(a p) t -> p a t", p=128)
                    dst3 = xt_sb[xb][:, :].rearrange(
                        "p (a t) -> p a t", t=512)[:, :, 0:w]
                    xtw = [(VV, xnlast_vv.get(ci - 1, 0)),
                           (SV, sq_sv_d.get(ci - 1, 0))]
                    if ci == 0:
                        xtw += [(VV, last_dq_cast), (DG, dequant_dg)]
                    sch.add("sync", lambda e, s=src, d=dst3: dma(e, d, s),
                            waits=xtw, inc=(DX, 16))
                    x_loaded[ci] = DX.n

                    for k in range(8):
                        sl = xt_sb[xb][:, :].rearrange(
                            "p (a t) -> p a t", t=512)[:, k, 0:w]
                        wts = []
                        if k == 0:
                            wts = [(DX, max(x_loaded[ci], init_dx)),
                                   (SV, xp_sv_d.get(ci - 1, 0))]
                        sch.add("tensor", lambda e, sl=sl, k=k, w=w: e.matmul(
                            quad1[0:1, 0:w], onec_sb[:, :], sl,
                            start=(k == 0), stop=(k == 7)),
                            waits=wts, inc=(TM, 1) if k == 7 else None)
                    sch.add("scalar", lambda e, xb=xb: e.activation(
                        xsq[:, :], xt_sb[xb][:, :], AF.Square),
                        waits=[(DX, x_loaded[ci]),
                               (TM, sumsq_tm.get(ci - 1, 0))], inc=(SV, 1))
                    sq_sv = SV.n
                    sq_sv_d[ci] = SV.n
                    for k in range(8):
                        sq = xsq[:, :].rearrange(
                            "p (a t) -> p a t", t=512)[:, k, 0:w]
                        sch.add("tensor", lambda e, sq=sq, k=k, w=w: e.matmul(
                            quad1[0:1, 512:512 + w], onec_sb[:, :], sq,
                            start=(k == 0), stop=(k == 7)),
                            waits=[(SV, sq_sv)] if k == 0 else [],
                            inc=(TM, 1) if k == 7 else None)
                    sumsq_tm[ci] = TM.n
                    sums_tm = TM.n

                    def ln_rows(e, w=w):
                        e.tensor_scalar(mrow[:, 0:w], quad1[0:1, 0:w],
                                        1.0 / DM, None, ALU.mult)
                        e.tensor_scalar(vrow[:, 0:w], quad1[0:1, 512:512 + w],
                                        1.0 / DM, None, ALU.mult)
                        e.tensor_mul(rrow[:, 0:w], mrow[:, 0:w], mrow[:, 0:w])
                        return e.tensor_sub(vrow[:, 0:w], vrow[:, 0:w],
                                            rrow[:, 0:w])
                    sch.add("vector", ln_rows, waits=[(TM, sums_tm)],
                            inc=(VV, 1))
                    var_vv = VV.n

                    def ln_rstd(e, w=w):
                        e.activation(rrow[:, 0:w], vrow[:, 0:w], AF.Ln,
                                     bias=eps_sb[0:1, 0:1])
                        return e.activation(rrow[:, 0:w], rrow[:, 0:w],
                                            AF.Exp, scale=-0.5)
                    sch.add("scalar", ln_rstd, waits=[(VV, var_vv)],
                            inc=(SV, 1))
                    rstd_sv = SV.n
                    sch.add("tensor", lambda e, w=w: e.matmul(
                        quad1[:, 1024:1024 + w], oner_sb[:, :], mrow[:, 0:w],
                        start=True, stop=True), waits=[(VV, var_vv)])
                    sch.add("tensor", lambda e, w=w: e.matmul(
                        quad1[:, 1536:1536 + w], oner_sb[:, :], rrow[:, 0:w],
                        start=True, stop=True), waits=[(SV, rstd_sv)],
                        inc=(TM, 1))
                    bc_tm = TM.n

                    for k in range(8):
                        sl = xt_sb[xb][:, :].rearrange(
                            "p (a t) -> p a t", t=512)[:, k, 0:w]
                        xn_slice = xn_sb[:, k * TA + c0:k * TA + c0 + w]

                        def xn_ops(e, sl=sl, xn_slice=xn_slice, w=w):
                            e.tensor_sub(tmp_f[:, 0:w], sl,
                                         quad1[:, 1024:1024 + w])
                            return e.tensor_mul(xn_slice, tmp_f[:, 0:w],
                                                quad1[:, 1536:1536 + w])
                        sch.add("vector", xn_ops,
                                waits=[(TM, bc_tm)] if k == 0 else [],
                                inc=(VV, 1) if k == 7 else None)
                    xnlast_vv[ci] = VV.n
                    xn_vv = VV.n

                    for otl in range(32):
                        wb = otl % 2
                        wsrc = wit.ap()[ds(otl * 1024, 1024), :].rearrange(
                            "(a p) c -> p a c", p=128)
                        prev_cast = wicast_sv.get((ci, otl - 2), 0) or \
                            wicast_sv.get((ci - 1, otl + 30), 0)
                        sch.add("sync", lambda e, s=wsrc, wb=wb: dma(
                            e, wi_sb[wb][:, :].rearrange(
                                "p (a c) -> p a c", c=128), s),
                            waits=[(SV, prev_cast)], inc=(DX, 16))
                        wld = DX.n
                        prev_mm = wimm_tm.get((ci, otl - 2), 0) or \
                            wimm_tm.get((ci - 1, otl + 30), 0)
                        sch.add("scalar", lambda e, wb=wb: e.copy(
                            wib_sb[wb][:, :], wi_sb[wb][:, :]),
                            waits=[(DX, wld), (TM, prev_mm)], inc=(SV, 1))
                        wicast_sv[(ci, otl)] = SV.n
                        wcast = SV.n
                        slot = otl % 4
                        ps = quad0[:, slot * 512:slot * 512 + w]
                        prev_ev = evac_sv.get((ci, otl - 4), 0) or \
                            evac_sv.get((ci - 1, otl + 28), 0)
                        for k in range(8):
                            rhs = xn_sb[:, k * TA + c0:k * TA + c0 + w]
                            wts = []
                            if k == 0:
                                wts = [(VV, xn_vv), (SV, wcast),
                                       (SV, prev_ev)]
                            sch.add("tensor", lambda e, ps=ps, wb=wb, k=k,
                                    rhs=rhs: e.matmul(
                                        ps, wib_sb[wb][:, k * 128:(k + 1) * 128],
                                        rhs, start=(k == 0), stop=(k == 7)),
                                    waits=wts,
                                    inc=(TM, 1) if k == 7 else None)
                        wimm_tm[(ci, otl)] = TM.n
                        mm_tm = TM.n
                        if otl < 16:
                            dstt = xin_sb[:, otl * 512:otl * 512 + w]
                            wts = [(TM, mm_tm)]
                            if otl == 0:
                                wts.append((VV, convlast_vv.get(ci - 1, 0)))
                            sch.add("scalar", lambda e, ps=ps, d=dstt: e.copy(
                                d, ps), waits=wts, inc=(SV, 1))
                        else:
                            dstt = zs_sb[:, (otl - 16) * 512:
                                         (otl - 16) * 512 + w]
                            wts = [(TM, mm_tm)]
                            if otl == 16:
                                wts.append((DG, sp_dg.get(("zs", ci - 1), 0)))
                            sch.add("scalar", lambda e, ps=ps, d=dstt:
                                    e.activation(d, ps, AF.Silu),
                                    waits=wts, inc=(SV, 1))
                        evac_sv[(ci, otl)] = SV.n
                    xz_sv = SV.n

                    for dt16 in range(16):
                        def conv_ops(e, dt16=dt16, w=w, ab=dt16 % 2):
                            e.tensor_copy(xe_sb[:, 0:3],
                                          tail_sb[:, dt16 * 3:dt16 * 3 + 3])
                            e.tensor_copy(xe_sb[:, 3:3 + w],
                                          xin_sb[:, dt16 * 512:dt16 * 512 + w])
                            e.tensor_copy(tail_sb[:, dt16 * 3:dt16 * 3 + 3],
                                          xin_sb[:, dt16 * 512 + w - 3:
                                                 dt16 * 512 + w])
                            e.tensor_scalar(acc_sb[ab][:, 0:w], xe_sb[:, 0:w],
                                            cw_sb[:, dt16, 0:1], None,
                                            ALU.mult)
                            ins = None
                            for j in (1, 2, 3):
                                ins = e.scalar_tensor_tensor(
                                    acc_sb[ab][:, 0:w], xe_sb[:, j:j + w],
                                    cw_sb[:, dt16, j:j + 1], acc_sb[ab][:, 0:w],
                                    ALU.mult, ALU.add)
                            return ins
                        wts = [(SV, silu_sv.get((ci, dt16 - 2), 0) or
                                silu_sv.get((ci - 1, dt16 + 14), 0))]
                        if dt16 == 0:
                            wts.append((SV, xz_sv))
                        sch.add("vector", conv_ops, waits=wts, inc=(VV, 1))
                        conv_vv[(ci, dt16)] = VV.n
                        wts2 = [(VV, conv_vv[(ci, dt16)])]
                        if dt16 == 0:
                            wts2.append((DG, sp_dg.get(("xcv", ci - 1), 0)))
                        sch.add("scalar", lambda e, dt16=dt16, w=w,
                                ab=dt16 % 2: e.activation(
                                    xcv_sb[:, dt16 * 512:dt16 * 512 + w],
                                    acc_sb[ab][:, 0:w], AF.Silu,
                                    bias=cb_sb[:, dt16:dt16 + 1]),
                                waits=wts2, inc=(SV, 1))
                        silu_sv[(ci, dt16)] = SV.n
                    convlast_vv[ci] = conv_vv[(ci, 15)]
                    conv_sv = SV.n

                    for k in range(16):
                        wts = []
                        if k == 0:
                            wts = [(SV, conv_sv), (VV, VV.n)]
                        sch.add("tensor", lambda e, k=k, w=w: e.matmul(
                            quad1[0:80, 0:w], wx_bf[:, k * 80:(k + 1) * 80],
                            xcv_sb[:, k * 512:k * 512 + w],
                            start=(k == 0), stop=(k == 15)),
                            waits=wts, inc=(TM, 1) if k == 15 else None)
                    wx_tm = TM.n
                    sch.add("scalar", lambda e, w=w: e.copy(
                        xp_sb[:, 0:w], quad1[0:80, 0:w]),
                        waits=[(TM, wx_tm), (DG, sp_dg.get(("xp", ci - 1), 0))],
                        inc=(SV, 1))
                    xp_sv_d[ci] = SV.n

                    v0 = max(c0, EX)
                    if v0 < c0 + w:
                        off = v0 - c0
                        vw = c0 + w - v0
                        g0 = v0 - EX
                        xcv_src = xcv_sb[:, :].rearrange(
                            "p (a t) -> p a t", t=512)[:, :, off:off + vw]
                        xcv_dst = xcv_in.ap()[:, g0:g0 + vw].rearrange(
                            "(a p) t -> p a t", p=128)
                        sch.add("gpsimd", lambda e, s=xcv_src, d=xcv_dst:
                                dma(e, d, s), waits=[(SV, conv_sv)],
                                inc=(DG, 16))
                        sp_dg[("xcv", ci)] = DG.n
                        zs_src = zs_sb[:, :].rearrange(
                            "p (a t) -> p a t", t=512)[:, :, off:off + vw]
                        zs_dst = zs_pri.ap()[:, g0:g0 + vw].rearrange(
                            "(a p) t -> p a t", p=128)
                        sch.add("gpsimd", lambda e, s=zs_src, d=zs_dst:
                                dma(e, d, s), waits=[(SV, xz_sv)],
                                inc=(DG, 16))
                        sp_dg[("zs", ci)] = DG.n
                        sch.add("gpsimd", lambda e, off=off, vw=vw, g0=g0:
                                dma(e, xp_in.ap()[:, g0:g0 + vw],
                                    xp_sb[:, off:off + vw]),
                                waits=[(SV, xp_sv_d[ci])], inc=(DG, 16))
                        sp_dg[("xp", ci)] = DG.n
                        last_spill = DG.n

                pa_vv, pa_sv, pa_tm = VV.n, SV.n, TM.n

            hf_dump = [None]
            with ExitStack() as fctx:
                SBf = lambda n, s, d: fctx.enter_context(nc.sbuf_tensor(n, s, d))
                hf_sb = SBf("hf_sb", [128, 24 * 1024], BF16)
                hf_dump[0] = hf_sb
                gs_sb = [SBf(f"gs_sb{i}", [128, 1024], BF16) for i in range(2)]
                wg_sb = [SBf(f"wg_sb{i}", [128, 8 * 128], BF16)
                         for i in range(2)]
                wu_sb = [SBf(f"wu_sb{i}", [128, 8 * 128], BF16)
                         for i in range(2)]
                wdn_sb = [SBf(f"wdn_sb{i}", [128, 24 * 128], BF16)
                          for i in range(2)]

                hf_vv_d = {}
                gsil_sv = {}
                gdn_sv = {}
                dn_tm = {}
                first = [True]
                for pss in range(2):
                    po = pss * 1024
                    for ht in range(24):
                        hb = ht % 2
                        qd = quad0 if hb == 0 else quad1
                        for warr, wsb in ((wgbf, wg_sb), (wubf, wu_sb)):
                            wsrc = warr.ap()[ds(ht * 1024, 1024), :].rearrange(
                                "(a p) c -> p a c", p=128)
                            if first[0]:
                                ww = [(DG, max(last_spill, dequant_dg)),
                                      (VV, pa_vv), (SV, pa_sv), (TM, pa_tm)]
                                first[0] = False
                            else:
                                ww = [(VV, hf_vv_d.get((pss, ht - 2), 0) or
                                       (hf_vv_d.get((pss - 1, ht + 22), 0)))]
                                if DEBUG and len(first) > 1 and pss == 1 \
                                        and ht == 0:
                                    ww.append((DG, first[1]))
                            sch.add("sync", lambda e, s=wsrc, d=wsb[hb]:
                                    dma(e, d[:, :].rearrange(
                                        "p (a c) -> p a c", c=128), s),
                                    waits=ww, inc=(DX, 16))
                        wld = DX.n
                        pw = [(VV, hf_vv_d.get((pss, ht - 1), 0) or
                               hf_vv_d.get((pss - 1, ht + 23), 0)),
                              (SV, gsil_sv.get((pss, ht - 1), 0) or
                               gsil_sv.get((pss - 1, ht + 23), 0))]
                        if pss == 1 and ht < 2:
                            pw.append((SV, gdn_sv.get((0, 6 + ht), 0)))
                        for k in range(8):
                            for sub in range(2):
                                rhs = xn_sb[:, k * TA + EX + po + sub * 512:
                                            k * TA + EX + po + sub * 512 + 512]
                                sch.add("tensor", lambda e, qd=qd, sub=sub,
                                        k=k, rhs=rhs, hb=hb: e.matmul(
                                            qd[:, sub * 512:sub * 512 + 512],
                                            wg_sb[hb][:, k * 128:(k + 1) * 128],
                                            rhs, start=(k == 0),
                                            stop=(k == 7)),
                                        waits=([(DX, wld)] + pw)
                                        if (k == 0 and sub == 0) else [])
                                sch.add("tensor", lambda e, qd=qd, sub=sub,
                                        k=k, rhs=rhs, hb=hb: e.matmul(
                                            qd[:, 1024 + sub * 512:
                                               1536 + sub * 512],
                                            wu_sb[hb][:, k * 128:(k + 1) * 128],
                                            rhs, start=(k == 0),
                                            stop=(k == 7)),
                                        inc=(TM, 1)
                                        if (k == 7 and sub == 1) else None)
                        mm_tm = TM.n
                        if DEBUG and pss == 0 and ht == 4:
                            gcap = fctx.enter_context(nc.sbuf_tensor(
                                "gcap", [128, 2048], F32))
                            wcap = fctx.enter_context(nc.sbuf_tensor(
                                "wcap", [128, 1024], BF16))
                            sch.add("vector", lambda e, qd=qd: e.tensor_copy(
                                gcap[:, :], qd[:, 0:2048]),
                                waits=[(TM, mm_tm)], inc=(VV, 1))
                            gc_vv = VV.n
                            sch.add("vector", lambda e, hb=hb: e.tensor_copy(
                                wcap[:, :], wg_sb[hb][:, :]), inc=(VV, 1))
                            wc_vv = VV.n
                            sch.add("gpsimd", lambda e: dma(
                                e, dbg_gate.ap(), gcap[:, 0:1024]),
                                waits=[(VV, gc_vv)], inc=(DG, 16))
                            sch.add("gpsimd", lambda e: dma(
                                e, dbg_up.ap(), gcap[:, 1024:2048]),
                                inc=(DG, 16))
                            sch.add("gpsimd", lambda e: dma(
                                e, dbg_wg.ap(), wcap[:, :]),
                                waits=[(VV, wc_vv)], inc=(DG, 16))
                            gscap_dg = DG.n
                            sch.add("scalar", lambda e: e.copy(
                                gcap[0:1, 0:1], gcap[0:1, 0:1]),
                                waits=[(DG, gscap_dg)])
                        sch.add("scalar", lambda e, qd=qd, hb=hb: e.activation(
                            gs_sb[hb][:, :], qd[:, 0:1024], AF.Silu),
                            waits=[(TM, mm_tm),
                                   (VV, hf_vv_d.get((pss, ht - 2), 0) or
                                    hf_vv_d.get((pss - 1, ht + 22), 0))],
                            inc=(SV, 1))
                        sch.add("scalar", lambda e, hb=hb: e.copy(
                            gs_sb[hb][0:1, 0:4], gs_sb[hb][0:1, 0:4]),
                            inc=(SV, 1))
                        gsil_sv[(pss, ht)] = SV.n
                        sch.add("vector", lambda e, qd=qd, ht=ht, hb=hb:
                                e.tensor_mul(
                                    hf_sb[:, ht * 1024:(ht + 1) * 1024],
                                    gs_sb[hb][:, :], qd[:, 1024:2048]),
                                waits=[(SV, gsil_sv[(pss, ht)])], inc=(VV, 1))
                        hf_vv_d[(pss, ht)] = VV.n
                    hf_vv = VV.n
                    if DEBUG:
                        sch.add("gpsimd", lambda e, pss=pss: dma(
                            e, (dbg_hf if pss else dbg_hf0).ap(), hf_sb[:, :]),
                            waits=[(VV, hf_vv)], inc=(DG, 16))
                        hfdump_dg = DG.n
                    for mt in range(8):
                        mb = mt % 2
                        qd = quad0 if mb == 0 else quad1
                        wsrc = wdbf.ap()[ds(mt * 3072, 3072), :].rearrange(
                            "(a p) c -> p a c", p=128)
                        ww = [(TM, dn_tm.get((pss, mt - 2), 0) or
                               dn_tm.get((pss - 1, mt + 6), 0))]
                        sch.add("sync", lambda e, s=wsrc, d=wdn_sb[mb]: dma(
                            e, d[:, :].rearrange("p (a c) -> p a c", c=128),
                            s), waits=ww, inc=(DX, 16))
                        wld = DX.n
                        pw2 = [(SV, gdn_sv.get((pss, mt - 2), 0) or
                                gdn_sv.get((pss - 1, mt + 6), 0))]
                        if DEBUG and mt == 0:
                            pw2.append((DG, hfdump_dg))
                        if mt < 2:
                            pw2.append((VV, hf_vv))
                        for k in range(24):
                            for sub in range(2):
                                sch.add("tensor", lambda e, qd=qd, mb=mb, k=k,
                                        sub=sub: e.matmul(
                                            qd[:, sub * 512:sub * 512 + 512],
                                            wdn_sb[mb][:, k * 128:(k + 1) * 128],
                                            hf_sb[:, k * 1024 + sub * 512:
                                                  k * 1024 + sub * 512 + 512],
                                            start=(k == 0), stop=(k == 23)),
                                        waits=([(DX, wld), (VV, hf_vv)] + pw2)
                                        if (k == 0 and sub == 0) else [],
                                        inc=(TM, 1)
                                        if (k == 23 and sub == 1) else None)
                        dn_tm[(pss, mt)] = TM.n
                        sch.add("scalar", lambda e, qd=qd, mt=mt, po=po:
                                e.activation(
                                    g_sb[:, mt * TB + po:mt * TB + po + 1024],
                                    qd[:, 0:1024], AF.Silu),
                                waits=[(TM, dn_tm[(pss, mt)])], inc=(SV, 1))
                        gdn_sv[(pss, mt)] = SV.n
                    if DEBUG and pss == 0:
                        g0sv = SV.n
                        sch.add("gpsimd", lambda e: dma(
                            e, dbg_g0.ap(), g_sb[:, :]),
                            waits=[(SV, g0sv)], inc=(DG, 16))
                        g0dump_dg = DG.n
                        first.append(g0dump_dg)
                ffn_sv, ffn_vv, ffn_tm = SV.n, VV.n, TM.n
                if DEBUG:
                    sch.add("gpsimd", lambda e: dma(
                        e, dbg_g1.ap(), g_sb[:, :]),
                        waits=[(TM, ffn_tm), (SV, ffn_sv), (VV, ffn_vv)],
                        inc=(DG, 16))
                sch.add("gpsimd", lambda e: e.collective_compute(
                    "AllGather", ALU.bypass,
                    replica_groups=[list(range(NC))],
                    ins=[xcv_in.ap().opt()], outs=[xcv_g.ap().opt()]),
                    waits=[(DG, last_spill), (TM, ffn_tm), (SV, ffn_sv),
                           (VV, ffn_vv)], inc=(CC, 1))
                sch.add("gpsimd", lambda e: e.collective_compute(
                    "AllGather", ALU.bypass,
                    replica_groups=[list(range(NC))],
                    ins=[xp_in.ap().opt()], outs=[xp_g.ap().opt()]),
                    inc=(CC, 1))
                ag12_cc = CC.n

                def u_slice(e):
                    reg = nc.alloc_register(e.engine, "cid_g")
                    e.reg_load(reg, cv_sb[0:1, 0:1])
                    cid = e.snap(reg, min_val=0, max_val=NC - 1)
                    s = xcv_g.ap().rearrange("(rk d) t -> rk d t", rk=NC)[
                        :, ds(cid * DSL, DSL), :]
                    d = u_pri.ap().rearrange("(rk d) t -> rk d t", rk=NC)
                    return dma(e, d, s)
                sch.add("gpsimd", u_slice, waits=[(CC, ag12_cc)],
                        inc=(DG, 16))
                u_pri_dg = DG.n
            xnctx.close()

            with ExitStack() as sctx:
                SBs = lambda n, s, d: sctx.enter_context(nc.sbuf_tensor(n, s, d))
                u_sb = [SBs(f"u_sb{i}", [128, SCT], BF16) for i in range(2)]
                dt_sb = [SBs(f"dt_sb{i}", [128, SCT], F32) for i in range(2)]
                w_sb = [SBs(f"w_sb{i}", [128, SCT], BF16) for i in range(2)]
                sp_sb = [SBs(f"sp_sb{i}", [128, SCT], F32) for i in range(2)]
                sp2_sb = [SBs(f"sp2_sb{i}", [128, SCT], F32) for i in range(2)]
                a_sb = [SBs(f"a_sb{i}", [128, SCT], BF16) for i in range(4)]
                be_sb = SBs("be_sb", [128, SCT], BF16)
                h_sb = SBs("h_sb", [128, SCT], BF16)
                hc_sb = [SBs(f"hc_sb{i}", [128, SCT], BF16) for i in range(2)]
                bb_sb = [SBs(f"bb_sb{i}", [128, SCT], BF16) for i in range(2)]
                cc_sb = [SBs(f"cc_sb{i}", [128, SCT], BF16) for i in range(2)]
                b16 = SBs("b16", [16, SCT], F32)
                c16 = SBs("c16", [16, SCT], F32)
                b16b = SBs("b16b", [16, SCT], BF16)
                c16b = SBs("c16b", [16, SCT], BF16)
                dtr_sb = SBs("dtr_sb", [1, SCT], F32)
                hcar = SBs("hcar", [128, 32], F32)
                ysl = [SBs(f"ysl{i}", [128, SCT], BF16) for i in range(2)]

                sch.add("vector", lambda e: e.memset(hcar[:, :], 0.0),
                        waits=[(TM, ffn_tm), (SV, ffn_sv)], inc=(VV, 1))

                yev_vv = {}
                ydma_dg = {}
                aact_sv = {}
                scan_vv_d = {}
                ns_tm_d = {}
                dtb_s_sv = {}
                dtb_x_vv = {}
                dtb_v_vv = {}
                cev_vv = {}
                bev_sv = {}
                alast_sv = {}

                for ch in range(NSC):
                    rk, half = ch // 2, ch % 2
                    colr = half * SCT
                    for dtl in range(2):
                        def u_load(e, rk=rk, dtl=dtl, colr=colr):
                            row0 = rk * DSL + dtl * 128
                            return dma(e, u_sb[dtl][:, :],
                                       u_pri.ap()[row0:row0 + 128,
                                                  colr:colr + SCT])
                        uw = [(DG, u_pri_dg), (VV, yev_vv.get(ch - 1, 0))]
                        if ch == 0:
                            uw += [(TM, ffn_tm), (SV, ffn_sv), (VV, ffn_vv)]
                        sch.add("sync", u_load, waits=uw, inc=(DX, 16))
                    u_ld = DX.n
                    sch.add("sync", lambda e, rk=rk, colr=colr: dma(
                        e, b16[:, :],
                        xp_g.ap()[rk * 80:rk * 80 + 16, colr:colr + SCT]),
                        inc=(DX, 16))
                    sch.add("sync", lambda e, rk=rk, colr=colr: dma(
                        e, c16[:, :],
                        xp_g.ap()[rk * 80 + 32:rk * 80 + 48, colr:colr + SCT]),
                        inc=(DX, 16))
                    sch.add("sync", lambda e, rk=rk, colr=colr: dma(
                        e, dtr_sb[:, :],
                        xp_g.ap()[rk * 80 + 64:rk * 80 + 65, colr:colr + SCT]),
                        inc=(DX, 16))
                    xp_ld = DX.n
                    sch.add("vector", lambda e: e.tensor_copy(
                        b16b[:, :], b16[:, :]), waits=[(DX, xp_ld)])
                    sch.add("vector", lambda e: e.tensor_copy(
                        c16b[:, :], c16[:, :]), inc=(VV, 1))
                    bc16_vv = VV.n

                    for sub in range(2):
                        wts = []
                        if sub == 0:
                            wts = [(DX, xp_ld),
                                   (VV, dtb_x_vv.get((ch - 1, 1), 0)),
                                   (SV, ffn_sv)]
                            if ch == 0:
                                wts.append((VV, ffn_vv))
                        sch.add("tensor", lambda e, sub=sub: e.matmul(
                            quad1[:, 1024 + sub * 512:1536 + sub * 512],
                            oner_sb[:, :],
                            dtr_sb[:, sub * 512:(sub + 1) * 512],
                            start=True, stop=True),
                            waits=wts, inc=(TM, 1) if sub == 1 else None)
                    dtr_tm = TM.n
                    for dtl in range(2):
                        def dt_x(e, dtl=dtl):
                            e.tensor_scalar(sp_sb[dtl][:, 0:512],
                                            quad1[:, 1024:1536],
                                            wdt_sb[:, dtl:dtl + 1],
                                            bdt_sb[:, dtl:dtl + 1],
                                            ALU.mult, ALU.add)
                            return e.tensor_scalar(sp_sb[dtl][:, 512:1024],
                                                   quad1[:, 1536:2048],
                                                   wdt_sb[:, dtl:dtl + 1],
                                                   bdt_sb[:, dtl:dtl + 1],
                                                   ALU.mult, ALU.add)
                        sch.add("vector", dt_x,
                                waits=[(TM, dtr_tm),
                                       (SV, dtb_s_sv.get((ch - 1, dtl), 0))],
                                inc=(VV, 1))
                        dtb_x_vv[(ch, dtl)] = VV.n

                        def dt_s(e, dtl=dtl):
                            e.activation(sp2_sb[dtl][:, :], sp_sb[dtl][:, :],
                                         AF.Abs)
                            e.activation(sp2_sb[dtl][:, :], sp2_sb[dtl][:, :],
                                         AF.Exp, scale=-1.0)
                            return e.activation(sp2_sb[dtl][:, :],
                                                sp2_sb[dtl][:, :], AF.Ln,
                                                bias=1.0)
                        sch.add("scalar", dt_s,
                                waits=[(VV, dtb_x_vv[(ch, dtl)])],
                                inc=(SV, 1))
                        dtb_s_sv[(ch, dtl)] = SV.n

                        def dt_v(e, dtl=dtl):
                            e.tensor_relu(dt_sb[dtl][:, :], sp_sb[dtl][:, :])
                            e.tensor_add(dt_sb[dtl][:, :], dt_sb[dtl][:, :],
                                         sp2_sb[dtl][:, :])
                            return e.tensor_mul(w_sb[dtl][:, :],
                                                dt_sb[dtl][:, :],
                                                u_sb[dtl][:, :])
                        sch.add("vector", dt_v,
                                waits=[(SV, dtb_s_sv[(ch, dtl)]), (DX, u_ld),
                                       (SV, alast_sv.get(ch - 1, 0))],
                                inc=(VV, 1))
                        dtb_v_vv[(ch, dtl)] = VV.n
                    w_vv = VV.n

                    for n in range(DS):
                        nb = n % 2
                        for sub in range(2):
                            wts = []
                            if sub == 0:
                                wts = [(VV, bc16_vv),
                                       (VV, cev_vv.get((ch, n - 1), 0) or
                                        cev_vv.get((ch - 1, DS - 1), 0))]
                            sch.add("tensor", lambda e, n=n, sub=sub: e.matmul(
                                quad1[:, sub * 512:(sub + 1) * 512],
                                oneh_bf[:, n * 128:(n + 1) * 128],
                                b16b[:, sub * 512:(sub + 1) * 512],
                                start=True, stop=True),
                                waits=wts, inc=(TM, 1) if sub == 1 else None)
                        bmm = TM.n
                        sch.add("scalar", lambda e, nb=nb: e.copy(
                            bb_sb[nb][:, :], quad1[:, 0:1024]),
                            waits=[(TM, bmm),
                                   (VV, scan_vv_d.get((ch, n - 2, 1), 0) or
                                    scan_vv_d.get((ch - 1, DS - 1, 1), 0))],
                            inc=(SV, 1))
                        bev_sv[(ch, n)] = SV.n
                        for sub in range(2):
                            sch.add("tensor", lambda e, n=n, sub=sub: e.matmul(
                                quad1[:, sub * 512:(sub + 1) * 512],
                                oneh_bf[:, n * 128:(n + 1) * 128],
                                c16b[:, sub * 512:(sub + 1) * 512],
                                start=True, stop=True),
                                waits=[(SV, bev_sv[(ch, n)])]
                                if sub == 0 else [],
                                inc=(TM, 1) if sub == 1 else None)
                        cmm = TM.n
                        sch.add("vector", lambda e, nb=nb: e.tensor_copy(
                            cc_sb[nb][:, :], quad1[:, 0:1024]),
                            waits=[(TM, cmm)], inc=(VV, 1))
                        cev_vv[(ch, n)] = VV.n
                        for dtl in range(2):
                            idx = dtl * 16 + n
                            ab = (2 * (n % 2)) + dtl
                            sch.add("scalar", lambda e, dtl=dtl, idx=idx,
                                    ab=ab: e.activation(
                                        a_sb[ab][:, :], dt_sb[dtl][:, :],
                                        AF.Exp,
                                        scale=an_sb[:, idx:idx + 1]),
                                    waits=[(VV, w_vv),
                                           (VV, scan_vv_d.get(
                                               (ch, n - 2, dtl), 0) or
                                            scan_vv_d.get(
                                                (ch - 1, DS - 1, dtl), 0))],
                                    inc=(SV, 1))
                            a_sv = SV.n

                            def scan_ops(e, dtl=dtl, idx=idx, nb=nb, ab=ab,
                                         ch=ch):
                                e.tensor_mul(be_sb[:, :], w_sb[dtl][:, :],
                                             bb_sb[nb][:, :])
                                e.tensor_tensor_scan(
                                    h_sb[:, :], a_sb[ab][:, :], be_sb[:, :],
                                    0.0 if ch == 0 else hcar[:, idx:idx + 1],
                                    ALU.mult, ALU.add)
                                e.tensor_mul(hc_sb[dtl][:, :],
                                             h_sb[:, :], cc_sb[nb][:, :])
                                return e.tensor_copy(hcar[:, idx:idx + 1],
                                                     h_sb[:, SCT - 1:SCT])
                            prev_ns = ns_tm_d.get((ch, n - 1, dtl), 0) or \
                                ns_tm_d.get((ch - 1, DS - 1, dtl), 0)
                            sch.add("vector", scan_ops,
                                    waits=[(SV, a_sv), (TM, prev_ns),
                                           (VV, scan_vv_d.get(
                                               (ch - 1, n, dtl), 0))],
                                    inc=(VV, 1))
                            scan_vv_d[(ch, n, dtl)] = VV.n
                            if DEBUG and ch == 0 and n == 0 and dtl == 0:
                                svv = VV.n
                                for dd, ss in ((dbg_dt0, dt_sb[0][:, :]),
                                               (dbg_a0, a_sb[ab][:, :]),
                                               (dbg_bb0, bb_sb[nb][:, :]),
                                               (dbg_u0, u_sb[0][:, :]),
                                               (dbg_h0, h_sb[:, :])):
                                    sch.add("gpsimd",
                                            lambda e, d=dd, s=ss: dma(
                                                e, d.ap(), s),
                                            waits=[(VV, svv)], inc=(DG, 16))
                            for sub in range(2):
                                wts = []
                                if sub == 0:
                                    wts = [(VV, scan_vv_d[(ch, n, dtl)])]
                                    if n == 0 and dtl == 0:
                                        wts.append(
                                            (VV, yev_vv.get(ch - 1, 0)))
                                sch.add("tensor", lambda e, dtl=dtl, sub=sub,
                                        n=n: e.matmul(
                                            quad0[:, dtl * 1024 + sub * 512:
                                                  dtl * 1024 + sub * 512 + 512],
                                            id_bf[:, :],
                                            hc_sb[dtl][:, sub * 512:
                                                       (sub + 1) * 512],
                                            start=(n == 0),
                                            stop=(n == DS - 1)),
                                        waits=wts,
                                        inc=(TM, 1) if sub == 1 else None)
                            ns_tm_d[(ch, n, dtl)] = TM.n
                    alast_sv[ch] = SV.n
                    ns_tm = TM.n

                    for dtl in range(2):
                        sch.add("vector", lambda e, dtl=dtl: e.tensor_copy(
                            ysl[dtl][:, :],
                            quad0[:, dtl * 1024:(dtl + 1) * 1024]),
                            waits=[(TM, ns_tm),
                                   (DG, ydma_dg.get(ch - 1, 0))],
                            inc=(VV, 1))
                        yv = VV.n
                        sch.add("gpsimd", lambda e, dtl=dtl, rk=rk, colr=colr:
                                dma(e, y_in.ap()[dtl * 128:(dtl + 1) * 128,
                                                 rk * TB + colr:
                                                 rk * TB + colr + SCT],
                                    ysl[dtl][:, :]),
                                waits=[(VV, yv)], inc=(DG, 16))
                    yev_vv[ch] = VV.n
                    ydma_dg[ch] = DG.n

                sch.add("gpsimd", lambda e: e.collective_compute(
                    "AllGather", ALU.bypass,
                    replica_groups=[list(range(NC))],
                    ins=[y_in.ap().opt()], outs=[y_g.ap().opt()]),
                    waits=[(DG, ydma_dg[NSC - 1])], inc=(CC, 1))
                ag3_cc = CC.n

                def y_slice(e):
                    reg = nc.alloc_register(e.engine, "cid_g2")
                    e.reg_load(reg, cv_sb[0:1, 0:1])
                    cid = e.snap(reg, min_val=0, max_val=NC - 1)
                    return dma(e, y_pri.ap()[:, :],
                               y_g.ap()[:, ds(cid * TB, TB)])
                sch.add("gpsimd", y_slice, waits=[(CC, ag3_cc)], inc=(DG, 16))
                y_pri_dg = DG.n
                sc_vv, sc_sv, sc_tm = VV.n, SV.n, TM.n

            with ExitStack() as ectx:
                SBe = lambda n, s, d: ectx.enter_context(nc.sbuf_tensor(n, s, d))
                yt_sb = [SBe("yt_sb0", [128, 16 * 512], BF16)]
                xc2_sb = [SBe("xc2_sb0", [128, 16 * 512], BF16)]
                zs2_sb = [SBe("zs2_sb0", [128, 16 * 512], BF16)]
                y3_sb = [SBe("y3_sb0", [128, 16 * 512], BF16)]
                xt2_sb = [SBe("xt2_sb0", [128, 8 * 512], F32)]
                wo_sb = [SBe(f"wo_sb{i}", [128, 16 * 128], F32)
                         for i in range(2)]
                wob_sb = [SBe(f"wob_sb{i}", [128, 16 * 128], BF16)
                          for i in range(2)]
                p_sb = SBe("p_sb", [128, 512], BF16)
                oth_sb = [SBe("oth_sb0", [128, 8 * 512], F32)]

                yld_dg = {}
                wo_cast_sv = {}
                wo_mm_tm = {}
                y3_vv_d = {}
                pout_vv = {}
                out_dg = {}
                for sub in range(4):
                    c0 = sub * 512
                    eb = 0

                    def y_load(e, c0=c0, eb=eb):
                        return dma(e, yt_sb[eb][:, :].rearrange(
                            "p (a t) -> p a t", t=512),
                            y_pri.ap()[:, c0:c0 + 512].rearrange(
                                "(a p) t -> p a t", p=128))
                    lw = [(DG, y_pri_dg)]
                    if sub >= 1:
                        lw.append((VV, pout_vv.get(sub - 1, 0)))
                    sch.add("sync", y_load, waits=lw, inc=(DX, 16))
                    yld_dg[sub] = DX.n
                    sch.add("sync", lambda e, c0=c0, eb=eb: dma(
                        e, xc2_sb[eb][:, :].rearrange(
                            "p (a t) -> p a t", t=512),
                        xcv_in.ap()[:, c0:c0 + 512].rearrange(
                            "(a p) t -> p a t", p=128)), inc=(DX, 16))
                    sch.add("sync", lambda e, c0=c0, eb=eb: dma(
                        e, zs2_sb[eb][:, :].rearrange(
                            "p (a t) -> p a t", t=512),
                        zs_pri.ap()[:, c0:c0 + 512].rearrange(
                            "(a p) t -> p a t", p=128)), inc=(DX, 16))
                    sch.add("sync", lambda e, c0=c0, eb=eb: dma(
                        e, xt2_sb[eb][:, :].rearrange(
                            "p (a t) -> p a t", t=512),
                        xt.ap()[:, EX + c0:EX + c0 + 512].rearrange(
                            "(a p) t -> p a t", p=128)), inc=(DX, 16))
                    ld = DX.n

                    for dt16 in range(16):
                        def comb(e, dt16=dt16, eb=eb):
                            e.scalar_tensor_tensor(
                                y3_sb[eb][:, dt16 * 512:(dt16 + 1) * 512],
                                xc2_sb[eb][:, dt16 * 512:(dt16 + 1) * 512],
                                dp_sb[:, dt16:dt16 + 1],
                                yt_sb[eb][:, dt16 * 512:(dt16 + 1) * 512],
                                ALU.mult, ALU.add)
                            return e.tensor_mul(
                                y3_sb[eb][:, dt16 * 512:(dt16 + 1) * 512],
                                y3_sb[eb][:, dt16 * 512:(dt16 + 1) * 512],
                                zs2_sb[eb][:, dt16 * 512:(dt16 + 1) * 512])
                        wts = []
                        if dt16 == 0:
                            wts = [(DX, ld), (DX, yld_dg[sub]),
                                   (TM, wo_mm_tm.get((sub - 1, 7), 0))]
                        sch.add("vector", comb, waits=wts,
                                inc=(VV, 1) if dt16 == 15 else None)
                    y3_vv_d[sub] = VV.n

                    for mt in range(8):
                        mb = mt % 2
                        wsrc = wot.ap()[ds(mt * 2048, 2048), :].rearrange(
                            "(a p) c -> p a c", p=128)
                        ww = [(SV, wo_cast_sv.get((sub, mt - 2), 0) or
                               wo_cast_sv.get((sub - 1, mt + 6), 0))]
                        sch.add("sync", lambda e, s=wsrc, mb=mb: dma(
                            e, wo_sb[mb][:, :].rearrange(
                                "p (a c) -> p a c", c=128), s),
                            waits=ww, inc=(DX, 16))
                        wld = DX.n
                        sch.add("scalar", lambda e, mb=mb: e.copy(
                            wob_sb[mb][:, :], wo_sb[mb][:, :]),
                            waits=[(DX, wld),
                                   (TM, wo_mm_tm.get((sub, mt - 2), 0) or
                                    wo_mm_tm.get((sub - 1, mt + 6), 0))],
                            inc=(SV, 1))
                        wo_cast_sv[(sub, mt)] = SV.n
                        qd = quad0 if mb == 0 else quad1
                        slot = (mt // 2) % 2
                        ps = qd[:, slot * 512:(slot + 1) * 512]
                        pw = [(VV, pout_vv.get((sub, mt - 4), 0) or
                               pout_vv.get((sub - 1, mt + 4), 0))]
                        for k in range(16):
                            wts = []
                            if k == 0:
                                wts = [(VV, y3_vv_d[sub]),
                                       (SV, wo_cast_sv[(sub, mt)])] + pw
                            sch.add("tensor", lambda e, ps=ps, mb=mb, k=k,
                                    eb=eb: e.matmul(
                                        ps,
                                        wob_sb[mb][:, k * 128:(k + 1) * 128],
                                        y3_sb[eb][:, k * 512:(k + 1) * 512],
                                        start=(k == 0), stop=(k == 15)),
                                    waits=wts,
                                    inc=(TM, 1) if k == 15 else None)
                        wo_mm_tm[(sub, mt)] = TM.n

                        def pout(e, ps=ps, mt=mt, sub=sub, eb=eb, c0=c0):
                            e.tensor_mul(p_sb[:, :], ps,
                                         g_sb[:, mt * TB + c0:
                                              mt * TB + c0 + 512])
                            return e.tensor_add(
                                oth_sb[eb][:, mt * 512:(mt + 1) * 512],
                                xt2_sb[eb][:, mt * 512:(mt + 1) * 512],
                                p_sb[:, :])
                        wts = [(TM, wo_mm_tm[(sub, mt)])]
                        if mt == 0 and sub >= 1:
                            wts.append((DG, out_dg.get(sub - 1, 0)))
                        sch.add("vector", pout, waits=wts, inc=(VV, 1))
                        pout_vv[(sub, mt)] = VV.n
                    pout_vv[sub] = VV.n

                    sch.add("gpsimd", lambda e, c0=c0, eb=eb: dma(
                        e, ot.ap()[:, c0:c0 + 512].rearrange(
                            "(a p) t -> p a t", p=128),
                        oth_sb[eb][:, :].rearrange("p (a t) -> p a t", t=512)),
                        waits=[(VV, pout_vv[sub])], inc=(DG, 16))
                    out_dg[sub] = DG.n

                if DEBUG:
                    for dsts, srcs in ((dbg_xcv, xcv_in), (dbg_zs, zs_pri),
                                       (dbg_xp, xp_in), (dbg_y, y_in)):
                        sch.add("gpsimd", lambda e, d=dsts, s=srcs: dma(
                            e, d.ap(), s.ap()), waits=[(DG, out_dg[3])],
                            inc=(DG, 16))
                    sch.add("gpsimd", lambda e: dma(
                        e, dbg_g.ap(), g_sb[:, :]), inc=(DG, 16))
                    dbg_dg = DG.n
                    sch.add("gpsimd", lambda e: e.engine_nop(),
                            waits=[(DG, dbg_dg)])
                else:
                    sch.add("gpsimd", lambda e: e.engine_nop(),
                            waits=[(DG, out_dg[3])])

        with nc.Block() as block:
            @block.sync
            def _(eng):
                sch.emit(eng, "sync")

            @block.gpsimd
            def _(eng):
                sch.emit(eng, "gpsimd")

            @block.vector
            def _(eng):
                sch.emit(eng, "vector")

            @block.scalar
            def _(eng):
                sch.emit(eng, "scalar")

            @block.tensor
            def _(eng):
                sch.emit(eng, "tensor")

    return nc


_NC_CACHE = [None]
_LAST_RES = [None]


def kernel(L0_attn_x, L0_attn_W_in, L0_attn_conv_w, L0_attn_conv_b,
           L0_attn_W_x, L0_attn_W_dt, L0_attn_b_dt, L0_attn_A_log,
           L0_attn_D_param, L0_attn_W_out, L0_ffn_w_gate, L0_ffn_w_up,
           L0_ffn_w_down):
    x = np.asarray(L0_attn_x)
    xT = np.ascontiguousarray(x.reshape(L, DM).T)
    wiT = np.ascontiguousarray(np.asarray(L0_attn_W_in).T)
    wit_t = np.ascontiguousarray(
        wiT.reshape(8, 128, 32, 128).transpose(2, 0, 1, 3)).reshape(-1, 128)
    wx = np.asarray(L0_attn_W_x)
    wxt = np.zeros((DI, 80), np.float32)
    wxt[:, 0:16] = wx[0:16, :].T
    wxt[:, 32:48] = wx[16:32, :].T
    wxt[:, 64:65] = wx[32:33, :].T
    cw = np.ascontiguousarray(np.asarray(L0_attn_conv_w).reshape(DI, 4))
    cb = np.ascontiguousarray(np.asarray(L0_attn_conv_b).reshape(DI, 1))
    dpar = np.ascontiguousarray(np.asarray(L0_attn_D_param).reshape(DI, 1))
    wdt_full = np.asarray(L0_attn_W_dt).reshape(DI, 1)
    bdt_full = np.asarray(L0_attn_b_dt).reshape(DI, 1)
    alog_full = np.asarray(L0_attn_A_log)
    wg = np.asarray(L0_ffn_w_gate)
    wu = np.asarray(L0_ffn_w_up)
    wd = np.asarray(L0_ffn_w_down)
    wg_t = np.ascontiguousarray(
        wg.T.reshape(8, 128, 24, 128).transpose(2, 0, 1, 3)).reshape(-1, 128)
    wu_t = np.ascontiguousarray(
        wu.T.reshape(8, 128, 24, 128).transpose(2, 0, 1, 3)).reshape(-1, 128)
    wd_t = np.ascontiguousarray(
        wd.T.reshape(24, 128, 8, 128).transpose(2, 0, 1, 3)).reshape(-1, 128)
    woT = np.ascontiguousarray(np.asarray(L0_attn_W_out).T)
    wot_t = np.ascontiguousarray(
        woT.reshape(16, 128, 8, 128).transpose(2, 0, 1, 3)).reshape(-1, 128)
    ones_c = np.ones((128, 1), np.float32)
    ones_r = np.ones((1, 128), np.float32)
    ident = np.eye(128, dtype=np.float32)
    oneh = np.zeros((16, 16 * 128), np.float32)
    for n in range(16):
        oneh[n, n * 128:(n + 1) * 128] = 1.0

    if _NC_CACHE[0] is None:
        _NC_CACHE[0] = build_nc()
    nc = _NC_CACHE[0]

    in_maps = []
    for c in range(NC):
        xt_c = np.zeros((DM, TA), np.float32)
        t0 = c * TB
        xt_c[:, EX:] = xT[:, t0:t0 + TB]
        lo = max(0, t0 - EX)
        n_pre = t0 - lo
        if n_pre > 0:
            xt_c[:, EX - n_pre:EX] = xT[:, lo:t0]
        dsl = slice(c * DSL, (c + 1) * DSL)
        in_maps.append({
            "xt": xt_c,
            "wit": wit_t,
            "wxt": wxt,
            "cwt": cw, "cbt": cb, "dpar": dpar,
            "wdt": np.ascontiguousarray(wdt_full[dsl]),
            "bdt": np.ascontiguousarray(bdt_full[dsl]),
            "alog": np.ascontiguousarray(alog_full[dsl]),
            "wg_t": wg_t, "wu_t": wu_t, "wd_t": wd_t,
            "wot": wot_t,
            "ones_c": ones_c, "ones_r": ones_r, "ident": ident,
            "oneh": oneh,
            "cvec": np.array([[c]], np.int32),
        })

    res = run_bass_kernel_spmd(nc, in_maps, core_ids=list(range(NC)))
    _LAST_RES[0] = res
    out = np.empty((L, DM), np.float32)
    for c in range(NC):
        out[c * TB:(c + 1) * TB, :] = np.asarray(res.results[c]["ot"]).T
    return out.reshape(1, L, DM)
